# version 01
# baseline (speedup 1.0000x reference)
import numpy as np
import ml_dtypes
import concourse.bass as bass
import concourse.mybir as mybir
from concourse.bass_utils import run_bass_kernel_spmd

F32 = mybir.dt.float32
F32R = mybir.dt.float32r
BF16 = mybir.dt.bfloat16
AF = mybir.ActivationFunctionType
ALU = mybir.AluOpType
AX = mybir.AxisListType.X

NCORE = 8
T = 8192
B_BATCH = 2
L_SEQ = 4096
DM = 4096
MTOK = 1024
DSL = 512
NST = 16
RDT = 128
KEXP = 64
RLO = 16
KR = KEXP * RLO
LC = 1024
NCH = T // LC
EPS = 1e-5
SCALING = 2.0

NEG_BIG = -3.0e38


def build_program():
    nc = bass.Bass("TRN2", target_bir_lowering=False)

    x_tok = nc.dram_tensor("x_tok", [MTOK, DM], F32, kind="ExternalInput").ap()
    x_dsl = nc.dram_tensor("x_dsl", [T, DSL], F32, kind="ExternalInput").ap()
    xpw_t = nc.dram_tensor("xpw_t", [DM, 160], F32, kind="ExternalInput").ap()
    dtw_t = nc.dram_tensor("dtw_t", [RDT, DSL], F32, kind="ExternalInput").ap()
    dt_b = nc.dram_tensor("dt_b", [128, 4], F32, kind="ExternalInput").ap()
    a_log = nc.dram_tensor("a_log", [128, 64], F32, kind="ExternalInput").ap()
    dvec = nc.dram_tensor("dvec", [128, 4], F32, kind="ExternalInput").ap()
    ffnA_t = nc.dram_tensor("ffnA_t", [DM, KR], F32, kind="ExternalInput").ap()
    ffnB_sl = nc.dram_tensor("ffnB_sl", [KR, DSL], F32, kind="ExternalInput").ap()
    wr_t = nc.dram_tensor("wr_t", [DM, KEXP], F32, kind="ExternalInput").ap()
    ident_in = nc.dram_tensor("ident", [128, 128], F32, kind="ExternalInput").ap()
    ident_bf_in = nc.dram_tensor("ident_bf", [128, 128], BF16, kind="ExternalInput").ap()
    epat_in = nc.dram_tensor("epat", [KEXP, 8 * 128], F32, kind="ExternalInput").ap()
    out_sl = nc.dram_tensor("out_sl", [T, DSL], F32, kind="ExternalOutput").ap()

    cc1_src = nc.dram_tensor("cc1_src", [162, MTOK], F32)
    cc1_dst = nc.dram_tensor("cc1_dst", [8 * 162, MTOK], F32)
    cc2_src = nc.dram_tensor("cc2_src", [KR, MTOK], BF16)
    cc2_dst = nc.dram_tensor("cc2_dst", [8 * KR, MTOK], BF16)
    g1 = cc1_dst.ap()
    g2 = cc2_dst.ap()

    from contextlib import ExitStack
    with ExitStack() as stack:
        def sb(name, shape, dt=F32, **kw):
            return stack.enter_context(nc.sbuf_tensor(name, shape, dt, **kw))

        ident = sb("identt", [128, 128])
        ident_bf = sb("identtb", [128, 128], BF16)
        ident_r = sb("identr", [128, 128], F32R)
        A_scl = sb("A_scl", [128, 64])
        bias_dt = sb("bias_dt", [128, 4])
        D_t = sb("D_t", [128, 4])
        epat_r = sb("epat_r", [KEXP, 8 * 128], F32R)
        state = sb("state", [128, 64])
        stats_T = sb("stats_T", [128, 64 * 2])

        BIG1 = sb("BIG1", [128, 16384])
        BIG2 = sb("BIG2", [128, 8192])
        BIG3 = sb("BIG3", [128, 4096])
        BFA = sb("BFA", [128, 8192], BF16)
        BFB = sb("BFB", [128, 8192], BF16)
        BFC = sb("BFC", [128, 4096], BF16)

        xn_fm = BIG1[:].bitcast(BF16)
        xt = BIG2[:, 0:DM]
        sq = BIG2[:, DM:2 * DM]
        w_f32 = BIG1[:, 1024:6144]
        wr_f32 = BIG1[:, 6144:8192]
        fa_f32 = BIG3[:, 0:4096]
        w_bf = BFA[:, 0:5120]
        wr_bf = BFA[:, 5120:7168]
        fa_bf = BFB[:, 0:4096]
        xn_bf = BFB[:, 4096:8192]
        hg_sb = BFA[:, 0:8192]
        red = sb("red", [128, 8])
        cst = sb("cst", [128, 4])
        mu_all = sb("mu_all", [128, 8])
        rstd_all = sb("rstd_all", [128, 8])
        SH1 = sb("SH1", [128, 1024])
        SH2 = sb("SH2", [128, 1024])
        SH3 = sb("SH3", [128, 1024])
        SH4 = sb("SH4", [128, 1024])
        SH5 = sb("SH5", [128, 1024])
        SH6 = sb("SH6", [128, 1024])
        dt_in_sb = SH1[:, :]
        sc_sb = SH2[0:KEXP, :]
        he_sb = SH3[:, :]
        g_t_t = sb("g_t_t", [KEXP, MTOK], F32R)
        g_t = g_t_t[:, :]
        bc_sb = SH5[0:32, :]
        pay_st = SH6[0:2, :]
        epat_f = BIG1[:, 0:1024]
        sc_t = sb("sc_t", [128, 8 * KEXP])
        w0 = sb("w0", [128, KEXP])
        w1 = sb("w1", [128, KEXP])
        m16 = sb("m16", [128, 16])
        tmp1 = sb("tmp1", [128, KEXP])
        tmp2 = sb("tmp2", [128, KEXP])
        col1 = sb("col1", [128, 1])
        col2 = sb("col2", [128, 1])
        g_sb = sb("g_sb", [128, 8 * KEXP])

        dt_all = BIG1[:, 0:4 * LC]
        dtx_all = BIG1[:, 4 * LC:8 * LC]
        xn_fmd = BIG1[:, 8 * LC:12 * LC]
        y_sb = BIG1[:, 12 * LC:16 * LC]
        xd_tok = BIG2[:, 0:8 * DSL]
        xnd_tok = BIG2[:, 8 * DSL:16 * DSL]
        xy_sb = xnd_tok
        yT_sb = BIG3[:, 0:8 * DSL]
        hgc = BFA[:, 0:8192]
        fb_bf = BFB[:, 0:8 * DSL]
        di_bf = BFB[:, 4096:4096 + LC]
        sp1 = BFC[:].bitcast(F32)[:, 0:LC]
        sp2 = BFC[:].bitcast(F32)[:, LC:2 * LC]
        di_sb = SH6[:, :]
        dtw_f = sb("dtw_f", [128, DSL])
        dtw_bf = sb("dtw_bf", [128, DSL], BF16)
        fb_f32 = y_sb
        dxn_r_t = sb("dxn_r_t", [128, LC], F32R)
        dxn_r = dxn_r_t[:, :]
        cbc = SH1[:, :]
        bbc = SH2[:, :]
        dBx_b = SH3[:, :]
        h_b = SH4[:, :]
        dA_b = SH5[:, :]
        ch_b_t = sb("ch_b_t", [128, LC], F32R)
        ch_b = ch_b_t[:, :]
        st2 = SH6[0:2, :]
        xyb = sb("xyb", [128, DSL])
        fin = sb("fin", [128, DSL])

        PS = stack.enter_context(nc.psum_tensor("PS", [128, 4096], F32))
        S = stack.enter_context(nc.semaphore("S"))
        CC = stack.enter_context(nc.semaphore("CC"))
        block = stack.enter_context(nc.Block())

        prog = []
        def st(eng, inc, fn):
            prog.append((eng, inc, fn))
        def dma(fn):
            st("sync", 16, fn)
        def dve(fn):
            st("vector", 1, fn)
        def act(fn):
            st("scalar", 1, fn)
        def pe(fn):
            st("tensor", 1, fn)
        def pool(fn):
            st("gpsimd", 1, fn)

        def ps_at(col, width):
            return PS[:, col:col + width]

        dma(lambda e: e.dma_start(out=ident[:], in_=ident_in))
        dma(lambda e: e.dma_start(out=ident_bf[:], in_=ident_bf_in))
        dma(lambda e: e.dma_start(out=A_scl[:], in_=a_log))
        dma(lambda e: e.dma_start(out=bias_dt[:], in_=dt_b))
        dma(lambda e: e.dma_start(out=D_t[:], in_=dvec))
        dma(lambda e: e.dma_start(out=epat_f[0:KEXP, :], in_=epat_in))
        dve(lambda e: e.memset(cst[:, 0:1], EPS))
        dve(lambda e: e.memset(cst[:, 1:2], -0.5))
        dve(lambda e: e.memset(cst[:, 2:3], -1.0))
        dve(lambda e: e.tensor_copy(ident_r[:], ident[:]))
        dve(lambda e: e.tensor_copy(epat_r[:], epat_f[0:KEXP, :]))
        act(lambda e: e.activation(out=A_scl[:], in_=A_scl[:], func=AF.Exp))
        dve(lambda e: e.tensor_scalar_mul(A_scl[:], A_scl[:], -1.0))
        dma(lambda e: e.dma_start(out=w_f32.rearrange("p (k r) -> p k r", k=32),
                                  in_=xpw_t.rearrange("(k p) r -> p k r", p=128)))
        dve(lambda e: e.tensor_copy(w_bf[:], w_f32[:]))
        dma(lambda e: e.dma_start(out=wr_f32.rearrange("p (k r) -> p k r", k=32),
                                  in_=wr_t.rearrange("(k p) r -> p k r", p=128)))
        dve(lambda e: e.tensor_copy(wr_bf[:], wr_f32[:]))

        for tt in range(8):
            dma(lambda e, tt=tt: e.dma_start(out=xt[:], in_=x_tok[128 * tt:128 * tt + 128, :]))
            dve(lambda e: e.reduce_sum(red[:, 0:1], xt[:], axis=AX))
            dve(lambda e: e.tensor_mul(sq[:], xt[:], xt[:]))
            dve(lambda e: e.reduce_sum(red[:, 1:2], sq[:], axis=AX))
            dve(lambda e: e.tensor_scalar_mul(red[:, 2:3], red[:, 0:1], 1.0 / DM))
            dve(lambda e: e.tensor_scalar_mul(red[:, 3:4], red[:, 1:2], 1.0 / DM))
            dve(lambda e: e.tensor_mul(red[:, 4:5], red[:, 2:3], red[:, 2:3]))
            dve(lambda e: e.tensor_sub(red[:, 5:6], red[:, 3:4], red[:, 4:5]))
            act(lambda e: e.activation(out=red[:, 6:7], in_=red[:, 5:6], func=AF.Ln, bias=cst[:, 0:1]))
            act(lambda e: e.activation(out=red[:, 7:8], in_=red[:, 6:7], func=AF.Exp, scale=cst[:, 1:2]))
            dve(lambda e, tt=tt: e.tensor_copy(mu_all[:, tt:tt + 1], red[:, 2:3]))
            dve(lambda e, tt=tt: e.tensor_copy(rstd_all[:, tt:tt + 1], red[:, 7:8]))
            dve(lambda e: e.tensor_scalar(out=xn_bf[:], in0=xt[:], scalar1=red[:, 2:3],
                                          scalar2=red[:, 7:8], op0=ALU.subtract, op1=ALU.mult))
            for k in range(32):
                pe(lambda e, k=k: e.transpose(ps_at(0, 64).bitcast(BF16)[:, 0:128],
                                              xn_bf[:, 128 * k:128 * k + 128], ident_bf[:]))
                act(lambda e, k=k, tt=tt: e.activation(
                    out=xn_fm[:, k * MTOK + 128 * tt: k * MTOK + 128 * tt + 128],
                    in_=ps_at(0, 64).bitcast(BF16)[:, 0:128], func=AF.Identity))
            dve(lambda e, tt=tt: e.tensor_copy(red[:, 2:3], mu_all[:, tt:tt + 1]))
            dve(lambda e, tt=tt: e.tensor_copy(red[:, 3:4], rstd_all[:, tt:tt + 1]))
            pe(lambda e: e.transpose(PS[0:2, 128:256], red[:, 2:4], ident[:]))
            act(lambda e, tt=tt: e.activation(out=pay_st[:, 128 * tt:128 * tt + 128],
                                              in_=PS[0:2, 128:256], func=AF.Identity))

        for th in range(2):
            for k in range(32):
                pe(lambda e, th=th, k=k: e.matmul(
                    ps_at(0, 512), w_bf[:, k * 160: k * 160 + 128],
                    xn_fm[:, k * MTOK + 512 * th: k * MTOK + 512 * th + 512],
                    start=(k == 0), stop=(k == 31)))
            act(lambda e, th=th: e.activation(out=dt_in_sb[:, 512 * th:512 * th + 512],
                                              in_=ps_at(0, 512), func=AF.Identity))
            for k in range(32):
                pe(lambda e, th=th, k=k: e.matmul(
                    ps_at(512, 512)[0:32, :], w_bf[:, k * 160 + 128: k * 160 + 160],
                    xn_fm[:, k * MTOK + 512 * th: k * MTOK + 512 * th + 512],
                    start=(k == 0), stop=(k == 31)))
            act(lambda e, th=th: e.activation(out=bc_sb[:, 512 * th:512 * th + 512],
                                              in_=ps_at(512, 512)[0:32, :], func=AF.Identity))
            for k in range(32):
                pe(lambda e, th=th, k=k: e.matmul(
                    ps_at(1024, 512)[0:KEXP, :], wr_bf[:, k * 64: k * 64 + 64],
                    xn_fm[:, k * MTOK + 512 * th: k * MTOK + 512 * th + 512],
                    start=(k == 0), stop=(k == 31)))
            act(lambda e, th=th: e.activation(out=sc_sb[:, 512 * th:512 * th + 512],
                                              in_=ps_at(1024, 512)[0:KEXP, :], func=AF.Identity))

        dma(lambda e: e.dma_start(out=cc1_src.ap()[0:128, :], in_=dt_in_sb[:]))
        dma(lambda e: e.dma_start(out=cc1_src.ap()[128:160, :], in_=bc_sb[:]))
        dma(lambda e: e.dma_start(out=cc1_src.ap()[160:162, :], in_=pay_st[:]))
        st("gpsimd", 0, lambda e: e.collective_compute(
            "AllGather", ALU.bypass, replica_groups=[list(range(NCORE))],
            ins=[cc1_src.ap().opt()], outs=[cc1_dst.ap().opt()]).then_inc(CC))

        for tt in range(8):
            pe(lambda e, tt=tt: e.transpose(ps_at(0, 128)[0:128, 0:KEXP],
                                            sc_sb[:, 128 * tt:128 * tt + 128],
                                            ident[0:KEXP, 0:KEXP]))
            act(lambda e, tt=tt: e.activation(out=sc_t[:, tt * KEXP:(tt + 1) * KEXP],
                                              in_=ps_at(0, 128)[0:128, 0:KEXP], func=AF.Identity))
        for tt in range(8):
            s0 = lambda tt=tt: sc_t[:, tt * KEXP:(tt + 1) * KEXP]
            dve(lambda e, tt=tt: e.tensor_copy(w0[:], sc_t[:, tt * KEXP:(tt + 1) * KEXP]))
            dve(lambda e: e.max(m16[:, 0:8], w0[:]))
            dve(lambda e: e.match_replace(out=w1[:], in_to_replace=m16[:, 0:8],
                                          in_values=w0[:], imm_value=NEG_BIG))
            dve(lambda e: e.max(m16[:, 8:16], w1[:]))
            dve(lambda e: e.tensor_scalar_mul(col1[:], m16[:, 0:1], -1.0))
            dve(lambda e, tt=tt: e.tensor_scalar(out=tmp1[:], in0=sc_t[:, tt * KEXP:(tt + 1) * KEXP],
                                                 scalar1=m16[:, 15:16], scalar2=None, op0=ALU.is_ge))
            act(lambda e, tt=tt: e.activation(out=tmp2[:], in_=sc_t[:, tt * KEXP:(tt + 1) * KEXP],
                                              func=AF.Exp, bias=col1[:]))
            dve(lambda e: e.tensor_mul(tmp2[:], tmp2[:], tmp1[:]))
            dve(lambda e: e.reduce_sum(col2[:], tmp2[:], axis=AX))
            dve(lambda e: e.reciprocal(col1[:], col2[:]))
            dve(lambda e, tt=tt: e.tensor_scalar_mul(g_sb[:, tt * KEXP:(tt + 1) * KEXP],
                                                     tmp2[:], col1[:]))
        for tt in range(8):
            pe(lambda e, tt=tt: e.transpose(PS[0:KEXP, 0:128],
                                            g_sb[:, tt * KEXP:(tt + 1) * KEXP], ident[:]))
            dve(lambda e, tt=tt: e.tensor_copy(g_t[:, 128 * tt:128 * tt + 128],
                                               PS[0:KEXP, 0:128]))

        for j in range(8):
            dma(lambda e, j=j: e.dma_start(
                out=fa_f32.rearrange("p (k r) -> p k r", k=32),
                in_=ffnA_t[:, 128 * j:128 * j + 128].rearrange("(k p) r -> p k r", p=128)))
            dve(lambda e: e.tensor_copy(fa_bf[:], fa_f32[:]))
            for th in range(2):
                for k in range(32):
                    pe(lambda e, th=th, k=k: e.matmul(
                        ps_at(0, 512), fa_bf[:, k * 128: k * 128 + 128],
                        xn_fm[:, k * MTOK + 512 * th: k * MTOK + 512 * th + 512],
                        start=(k == 0), stop=(k == 31)))
                act(lambda e, th=th: e.activation(out=he_sb[:, 512 * th:512 * th + 512],
                                                  in_=ps_at(0, 512), func=AF.Identity))
                pe(lambda e, j=j, th=th: e.matmul(
                    ps_at(512, 512), epat_r[:, j * 128:(j + 1) * 128],
                    g_t[:, 512 * th:512 * th + 512], start=True, stop=True))
                dve(lambda e, j=j, th=th: e.tensor_mul(
                    hg_sb[:, j * MTOK + 512 * th: j * MTOK + 512 * th + 512],
                    he_sb[:, 512 * th:512 * th + 512], ps_at(512, 512)))
        dma(lambda e: e.dma_start(out=cc2_src.ap().rearrange("(j p) t -> p j t", p=128),
                                  in_=hg_sb[:].rearrange("p (j t) -> p j t", j=8)))
        st("gpsimd", 0, lambda e: e.collective_compute(
            "AllGather", ALU.bypass, replica_groups=[list(range(NCORE))],
            ins=[cc2_src.ap().opt()], outs=[cc2_dst.ap().opt()]).then_inc(CC))

        st("sync", 0, lambda e: e.wait_ge(CC, 1))
        for r in range(8):
            dma(lambda e, r=r: e.dma_start(out=st2[:], in_=g1[162 * r + 160:162 * r + 162, :]))
            for tt in range(8):
                pe(lambda e, tt=tt: e.transpose(ps_at(0, 128)[0:128, 0:2],
                                                st2[:, 128 * tt:128 * tt + 128],
                                                ident[0:2, 0:2]))
                act(lambda e, r=r, tt=tt: e.activation(
                    out=stats_T[:, (8 * r + tt) * 2:(8 * r + tt) * 2 + 2],
                    in_=ps_at(0, 128)[0:128, 0:2], func=AF.Identity))
        dma(lambda e: e.dma_start(out=dtw_f[:], in_=dtw_t))
        dve(lambda e: e.tensor_copy(dtw_bf[:], dtw_f[:]))
        dma(lambda e: e.dma_start(out=fb_f32[:].rearrange("p (j d) -> p j d", j=8),
                                  in_=ffnB_sl.rearrange("(j p) d -> p j d", p=128)))
        dve(lambda e: e.tensor_copy(fb_bf[:], fb_f32[:]))

        for c in range(NCH):
            dma(lambda e, c=c: e.dma_start(out=di_sb[:], in_=g1[162 * c:162 * c + 128, :]))
            dve(lambda e: e.tensor_copy(di_bf[:], di_sb[:]))
            for dt_i in range(4):
                for th in range(2):
                    pe(lambda e, dt_i=dt_i, th=th: e.matmul(
                        ps_at(0, 512), dtw_bf[:, 128 * dt_i:128 * dt_i + 128],
                        di_bf[:, 512 * th:512 * th + 512], start=True, stop=True))
                    dve(lambda e, dt_i=dt_i, th=th: e.tensor_scalar(
                        out=sp1[:, 512 * th:512 * th + 512], in0=ps_at(0, 512),
                        scalar1=bias_dt[:, dt_i:dt_i + 1], scalar2=0.0,
                        op0=ALU.add, op1=ALU.max))
                    act(lambda e, dt_i=dt_i, th=th: e.activation(
                        out=sp2[:, 512 * th:512 * th + 512], in_=ps_at(0, 512),
                        func=AF.Abs, bias=bias_dt[:, dt_i:dt_i + 1]))
                act(lambda e: e.activation(out=sp2[:], in_=sp2[:], func=AF.Exp, scale=cst[:, 2:3]))
                act(lambda e: e.activation(out=sp2[:], in_=sp2[:], func=AF.Ln, bias=1.0))
                dve(lambda e, dt_i=dt_i: e.tensor_add(dt_all[:, dt_i * LC:(dt_i + 1) * LC],
                                                      sp1[:], sp2[:]))
            dma(lambda e, c=c: e.dma_start(
                out=xd_tok[:].rearrange("p (s d) -> p s d", s=8),
                in_=x_dsl[LC * c:LC * (c + 1), :].rearrange("(s p) d -> p s d", p=128)))
            for s in range(8):
                g = 8 * c + s
                dve(lambda e, s=s, g=g: e.tensor_scalar(
                    out=xnd_tok[:, s * DSL:(s + 1) * DSL], in0=xd_tok[:, s * DSL:(s + 1) * DSL],
                    scalar1=stats_T[:, 2 * g:2 * g + 1], scalar2=stats_T[:, 2 * g + 1:2 * g + 2],
                    op0=ALU.subtract, op1=ALU.mult))
                for dt_i in range(4):
                    pe(lambda e, s=s, dt_i=dt_i: e.transpose(
                        ps_at(0, 128), xnd_tok[:, s * DSL + 128 * dt_i: s * DSL + 128 * dt_i + 128],
                        ident[:]))
                    act(lambda e, s=s, dt_i=dt_i: e.activation(
                        out=xn_fmd[:, dt_i * LC + 128 * s: dt_i * LC + 128 * s + 128],
                        in_=ps_at(0, 128), func=AF.Identity))
            dve(lambda e: e.tensor_mul(dtx_all[:], dt_all[:], xn_fmd[:]))

            for dt_i in range(4):
                dve(lambda e, dt_i=dt_i: e.tensor_scalar(
                    out=dxn_r[:], in0=xn_fmd[:, dt_i * LC:(dt_i + 1) * LC],
                    scalar1=D_t[:, dt_i:dt_i + 1], scalar2=None, op0=ALU.mult))
                for h2 in range(2):
                    pe(lambda e, dt_i=dt_i, h2=h2: e.matmul(
                        ps_at(dt_i * 1024 + 512 * h2, 512), ident_r[:],
                        dxn_r[:, 512 * h2:512 * h2 + 512], start=True, stop=False))
            for n in range(NST):
                dma(lambda e, c=c, n=n: e.dma_start(
                    out=bbc[:], in_=g1[162 * c + 128 + n:162 * c + 129 + n, :].partition_broadcast(128)))
                dma(lambda e, c=c, n=n: e.dma_start(
                    out=cbc[:], in_=g1[162 * c + 144 + n:162 * c + 145 + n, :].partition_broadcast(128)))
                for dt_i in range(4):
                    col = dt_i * 16 + n
                    act(lambda e, dt_i=dt_i, col=col: e.activation(
                        out=dA_b[:], in_=dt_all[:, dt_i * LC:(dt_i + 1) * LC],
                        func=AF.Exp, scale=A_scl[:, col:col + 1]))
                    pool(lambda e, dt_i=dt_i: e.tensor_mul(
                        dBx_b[:], dtx_all[:, dt_i * LC:(dt_i + 1) * LC], bbc[:]))
                    if c == 0 or c == NCH // 2:
                        dve(lambda e: e.tensor_tensor_scan(
                            out=h_b[:], data0=dA_b[:], data1=dBx_b[:], initial=0.0,
                            op0=ALU.mult, op1=ALU.add))
                    else:
                        dve(lambda e, col=col: e.tensor_tensor_scan(
                            out=h_b[:], data0=dA_b[:], data1=dBx_b[:],
                            initial=state[:, col:col + 1], op0=ALU.mult, op1=ALU.add))
                    dve(lambda e, col=col: e.tensor_copy(state[:, col:col + 1], h_b[:, LC - 1:LC]))
                    dve(lambda e: e.tensor_mul(ch_b[:], h_b[:], cbc[:]))
                    for h2 in range(2):
                        pe(lambda e, dt_i=dt_i, h2=h2, n=n: e.matmul(
                            ps_at(dt_i * 1024 + 512 * h2, 512), ident_r[:],
                            ch_b[:, 512 * h2:512 * h2 + 512], start=False, stop=(n == NST - 1)))
            for dt_i in range(4):
                act(lambda e, dt_i=dt_i: e.activation(
                    out=y_sb[:, dt_i * LC:(dt_i + 1) * LC], in_=ps_at(dt_i * 1024, 1024),
                    func=AF.Identity))
            for s in range(8):
                for dt_i in range(4):
                    pe(lambda e, s=s, dt_i=dt_i: e.transpose(
                        ps_at(0, 128), y_sb[:, dt_i * LC + 128 * s: dt_i * LC + 128 * s + 128],
                        ident[:]))
                    act(lambda e, s=s, dt_i=dt_i: e.activation(
                        out=yT_sb[:, s * DSL + 128 * dt_i: s * DSL + 128 * dt_i + 128],
                        in_=ps_at(0, 128), func=AF.Identity))
            dve(lambda e: e.tensor_add(xy_sb[:], xd_tok[:], yT_sb[:]))
            dma(lambda e, c=c: e.dma_start(
                out=out_sl[LC * c:LC * (c + 1), :].rearrange("(s p) d -> p s d", p=128),
                in_=xy_sb[:].rearrange("p (s d) -> p s d", s=8)))

        st("sync", 0, lambda e: e.wait_ge(CC, 2))
        for c in range(NCH):
            dma(lambda e, c=c: e.dma_start(
                out=hgc[:].rearrange("p (a t) -> p a t", a=8),
                in_=g2[KR * c:KR * (c + 1), :].rearrange("(a p) t -> p a t", p=128)))
            for s in range(8):
                for a in range(8):
                    pe(lambda e, s=s, a=a: e.matmul(
                        ps_at(0, 512), hgc[:, a * MTOK + 128 * s: a * MTOK + 128 * s + 128],
                        fb_bf[:, a * DSL:(a + 1) * DSL], start=(a == 0), stop=(a == 7)))
                dma(lambda e, c=c, s=s: e.dma_start(
                    out=xyb[:], in_=out_sl[LC * c + 128 * s: LC * c + 128 * s + 128, :]))
                dve(lambda e: e.scalar_tensor_tensor(
                    out=fin[:], in0=ps_at(0, 512), scalar=SCALING, in1=xyb[:],
                    op0=ALU.mult, op1=ALU.add))
                dma(lambda e, c=c, s=s: e.dma_start(
                    out=out_sl[LC * c + 128 * s: LC * c + 128 * s + 128, :], in_=fin[:]))

        def emit_for(eng_name, eng):
            n = 0
            for name, inc, fn in prog:
                if name == eng_name:
                    eng.wait_ge(S, n)
                    r = fn(eng)
                    if inc:
                        r.then_inc(S, inc)
                n += inc
            eng.wait_ge(S, n)

        @block.sync
        def _(e): emit_for("sync", e)
        @block.vector
        def _(e): emit_for("vector", e)
        @block.scalar
        def _(e): emit_for("scalar", e)
        @block.tensor
        def _(e): emit_for("tensor", e)
        @block.gpsimd
        def _(e): emit_for("gpsimd", e)

    return nc


_CACHE = {}


def kernel(L0_attn_x, L0_attn_A_log, L0_attn_D, L0_attn_x_proj_w,
           L0_attn_dt_proj_w, L0_attn_dt_proj_b, L0_ffn_A, L0_ffn_B, L0_ffn_Wr):
    x = np.ascontiguousarray(np.asarray(L0_attn_x, dtype=np.float32))
    X = x.reshape(T, DM)
    A_log = np.asarray(L0_attn_A_log, dtype=np.float32)
    Dv = np.asarray(L0_attn_D, dtype=np.float32)
    xpw = np.asarray(L0_attn_x_proj_w, dtype=np.float32)
    dtw = np.asarray(L0_attn_dt_proj_w, dtype=np.float32)
    dtb = np.asarray(L0_attn_dt_proj_b, dtype=np.float32)
    fA = np.asarray(L0_ffn_A, dtype=np.float32)
    fB = np.asarray(L0_ffn_B, dtype=np.float32)
    Wr = np.asarray(L0_ffn_Wr, dtype=np.float32)

    xpw_t = np.ascontiguousarray(xpw.T)
    ffnA_t = np.ascontiguousarray(fA.transpose(2, 0, 1).reshape(DM, KR))
    ffnB_kr = np.ascontiguousarray(fB.transpose(0, 2, 1).reshape(KR, DM))
    wr_t = np.ascontiguousarray(Wr.T)
    ident = np.eye(128, dtype=np.float32)
    ident_bf = np.eye(128, dtype=ml_dtypes.bfloat16)
    epat = np.zeros((KEXP, 8 * 128), dtype=np.float32)
    for j in range(8):
        for p in range(128):
            epat[8 * j + p // 16, j * 128 + p] = 1.0

    in_maps = []
    for i in range(NCORE):
        Di = slice(DSL * i, DSL * (i + 1))
        in_maps.append({
            "x_tok": np.ascontiguousarray(X[MTOK * i:MTOK * (i + 1), :]),
            "x_dsl": np.ascontiguousarray(X[:, Di]),
            "xpw_t": xpw_t,
            "dtw_t": np.ascontiguousarray(dtw[Di].T),
            "dt_b": np.ascontiguousarray(dtb[Di].reshape(4, 128).T),
            "a_log": np.ascontiguousarray(
                A_log[Di].reshape(4, 128, NST).transpose(1, 0, 2).reshape(128, 64)),
            "dvec": np.ascontiguousarray(Dv[Di].reshape(4, 128).T),
            "ffnA_t": ffnA_t,
            "ffnB_sl": np.ascontiguousarray(ffnB_kr[:, Di]),
            "wr_t": wr_t,
            "ident": ident,
            "ident_bf": ident_bf,
            "epat": epat,
        })

    if "nc" not in _CACHE:
        _CACHE["nc"] = build_program()
    nc = _CACHE["nc"]
    res = run_bass_kernel_spmd(nc, in_maps, core_ids=list(range(NCORE)))
    parts = [np.asarray(res.results[i]["out_sl"], dtype=np.float32) for i in range(NCORE)]
    out = np.concatenate(parts, axis=1)
    return out.reshape(B_BATCH, L_SEQ, DM)
